# version 01
# baseline (speedup 1.0000x reference)
import numpy as np
import concourse.bass as bass
import concourse.mybir as mybir
from concourse.bass_utils import run_bass_kernel_spmd

F32 = mybir.dt.float32
ALU = mybir.AluOpType
AF = mybir.ActivationFunctionType
AX = mybir.AxisListType

D = 1024
H = 16
HD = 64
TC = 2048
NA = 8
TA = 256
NF = 4
TF = 512
KC = 8
MASKC = 200.0
EPS = 1e-5
ENGS = ("sync", "gpsimd", "tensor", "vector", "scalar")


def _raw(t):
    return t.tensor if hasattr(t, "tensor") else t


class Build:

    def __init__(self, semnames):
        self.cnt = {k: 0 for k in semnames}
        self.q = {e: [] for e in ENGS}

    def W(self, eng, *toks):
        for t in toks:
            if t is not None:
                self.q[eng].append(("w", t[0], t[1]))

    def do(self, eng, maker, sem=None, n=1):
        if sem is None or n == 0:
            self.q[eng].append(("i", maker, None, 0))
            return None
        self.cnt[sem] += n
        self.q[eng].append(("i", maker, sem, n))
        return (sem, self.cnt[sem])

    def tok(self, sem):
        return (sem, self.cnt[sem])

    def replay(self, eng, E, sems):
        for item in self.q[eng]:
            if item[0] == "w":
                E.wait_ge(sems[item[1]], item[2])
            else:
                _, maker, sem, n = item
                inst = maker(E)
                if sem is not None:
                    inst.then_inc(sems[sem], n)


def build_nc():
    nc = bass.Bass("TRN2", target_bir_lowering=False, debug=False)

    d_in = {}

    def din(name, shape):
        d_in[name] = nc.dram_tensor(name, list(shape), F32, kind="ExternalInput").ap()
        return d_in[name]

    x_att = din("x_att", [TC, D])
    x_res = din("x_res", [TC, D])
    wqk = din("wqk", [16, 128, 1024])
    wv = din("wv", [8, 128, 1024])
    wproj = din("wproj", [8, 128, 1024])
    wup = din("wup", [8, 4, 128, 1024])
    wdown = din("wdown", [8, 4, 128, 1024])
    wsup = din("wsup", [16, 128, 1024])
    wsdown = din("wsdown", [8, 2, 128, 1024])
    wgate = din("wgate", [128, 64])
    bproj = din("bproj", [128, 8])
    gbias = din("gbias", [128, 8])
    identw = din("identw", [128, 128])
    kmask = din("kmask", [8, 16 * TA])
    qmask = din("qmask", [8, 16 * TA])
    selm = din("selm", [8, 1024])
    out_d = nc.dram_tensor("out", [TC, D], F32, kind="ExternalOutput").ap()

    from contextlib import ExitStack
    with ExitStack() as _st:
        acc = _st.enter_context(nc.sbuf_tensor("acc", [128, 8 * 2048], F32))
        ident = _st.enter_context(nc.sbuf_tensor("ident", [128, 128], F32))
        bproj_sb = _st.enter_context(nc.sbuf_tensor("bproj_sb", [128, 8], F32))
        gbias_sb = _st.enter_context(nc.sbuf_tensor("gbias_sb", [128, 8], F32))
        gate_sb = _st.enter_context(nc.sbuf_tensor("gate_sb", [128, 64], F32))
        selm_sb = _st.enter_context(nc.sbuf_tensor("selm_sb", [8, 1024], F32))
        wb = _st.enter_context(nc.sbuf_tensor("wb", [128, 2 * 1024], F32))
        wbig = _st.enter_context(nc.sbuf_tensor("wbig", [128, 2 * 1024], F32))
        outp = _st.enter_context(nc.sbuf_tensor("outp", [128, 2 * 1024], F32))
        stats = _st.enter_context(nc.sbuf_tensor("stats", [128, 48], F32))
        mv = _st.enter_context(nc.sbuf_tensor("mv", [128, 8], F32))
        rstd = _st.enter_context(nc.sbuf_tensor("rstd", [128, 4], F32))
        rzb = _st.enter_context(nc.sbuf_tensor("rzb", [128, 4], F32))
        ARF = 26624
        AR = _st.enter_context(nc.sbuf_tensor("AR", [128, ARF], F32))
        _off = [0]

        def take(parts, elems):
            o = _off[0]
            _off[0] += elems
            assert _off[0] <= ARF, "arena overflow"
            return AR[0:parts, o:o + elems], o
        xa, _ = take(128, 2 * 1024)
        xnf, _ = take(128, 8 * TA)
        wvb, _ = take(128, 2 * 1024)
        q_sr, _ = take(72, 16 * TA)
        k_sr, _ = take(72, 16 * TA)
        oddst, _ = take(128, 8 * TA)
        v_tok, c_vtok = take(128, 2 * 1024)
        v_pv, c_vpv = take(128, 2 * 16 * 65)
        Pt, _ = take(128, 512)
        o_sb, _ = take(128, 1024)
        o_mq, _ = take(128, 8 * TA)
        _off[0] = 0
        xr, _ = take(128, 4 * 1024)
        xnr, _ = take(128, 2 * 1024)
        xnff, _ = take(128, 8 * TF)
        lgt, _ = take(128, 32)
        rt0, _ = take(128, 32)
        rt1, _ = take(128, 64)
        rt2, _ = take(128, 32)
        rt3, _ = take(128, 32)
        rt4, _ = take(128, 32)
        rt5, _ = take(128, 32)
        cwfm, _ = take(128, TF)
        scr, _ = take(128, 8 * TF)
        routed, _ = take(128, 8 * TF)
        p0 = _st.enter_context(nc.psum_tensor("p0", [128, 512], F32))
        p1 = _st.enter_context(nc.psum_tensor("p1", [128, 512], F32))
        p2 = _st.enter_context(nc.psum_tensor("p2", [128, 512], F32))
        p3 = _st.enter_context(nc.psum_tensor("p3", [128, 512], F32))
        p4 = _st.enter_context(nc.psum_tensor("p4", [128, 512], F32))
        p5 = _st.enter_context(nc.psum_tensor("p5", [128, 512], F32))
        p6 = _st.enter_context(nc.psum_tensor("p6", [128, 512], F32))
        p7 = _st.enter_context(nc.psum_tensor("p7", [128, 512], F32))
        semnames = ["I", "X", "W0", "W1", "WV0", "WV1", "B0", "B1",
                    "SH", "PV", "O0", "O1", "T", "V", "S"]
        sems = {k: _st.enter_context(nc.semaphore("s" + k)) for k in semnames}
        block = _st.enter_context(nc.Block())
        bd = Build(list(sems))
        W = bd.W
        pmm = [p0, p1]
        pv4 = [p4, p5, p6, p7]
        pv4_last = [None] * 4

        for ap_src, dst in ((identw, ident), (bproj, bproj_sb), (gbias, gbias_sb),
                            (wgate, gate_sb)):
            bd.do("sync", lambda E, s=ap_src, d=dst: E.dma_start(out=d[:], in_=s), "I", 16)
        bd.do("sync", lambda E: E.dma_start(out=selm_sb[:], in_=selm), "I", 16)
        bd.do("sync", lambda E: E.dma_start(out=k_sr[64:72, :], in_=kmask), "I", 16)
        bd.do("sync", lambda E: E.dma_start(out=q_sr[64:72, :], in_=qmask), "I", 16)
        t_init = bd.tok("I")
        W("tensor", t_init)
        W("vector", t_init)
        W("scalar", t_init)
        v_ones = bd.do("vector", lambda E: E.memset(v_pv[:], 1.0), "V", 1)

        last = {
            "xa": None, "xnf": None, "q_sr": None, "k_sr": None, "oddst": None,
            "v_tok": None, "v_pv": None, "Pt": None, "o_sb": None, "o_mq": None,
            "pmm0": None, "pmm1": None, "p2": None, "p3": None,
            "xr": None, "xnr0": None, "xnr1": None, "xnff": None,
            "scr": None, "routed": None, "cwfm": None,
        }
        t_wb_prev = [None, None]
        t_wv_prev = [None, None]
        t_wbig_prev = [None, None]
        t_outp_prev = [None, None]

        def ln_subtile(src_sb, src_off, dst_sb, dst_off, st_i, v_deps):
            W("vector", *v_deps)
            bd.do("vector", lambda E, o=src_off, i=st_i:
                  E.bn_stats(stats[:, 12 * i:12 * i + 6], src_sb[:, o:o + 512]), "V", 1)
            t2 = bd.do("vector", lambda E, o=src_off, i=st_i:
                       E.bn_stats(stats[:, 12 * i + 6:12 * i + 12],
                                  src_sb[:, o + 512:o + 1024]), "V", 1)
            W("vector", t2)
            t3 = bd.do("vector", lambda E, i=st_i:
                       E.bn_aggr(mv[:, 2 * i:2 * i + 2], stats[:, 12 * i:12 * i + 12]),
                       "V", 1)
            W("vector", t3)
            t3b = bd.do("vector", lambda E, i=st_i:
                        E.tensor_scalar_add(rstd[:, i:i + 1],
                                            mv[:, 2 * i + 1:2 * i + 2], EPS), "V", 1)
            W("scalar", t3b)
            t4 = bd.do("scalar", lambda E, i=st_i:
                       E.activation(rstd[:, i:i + 1], rstd[:, i:i + 1], AF.Ln), "S", 1)
            W("vector", t4)
            t4b = bd.do("vector", lambda E, i=st_i:
                        E.tensor_scalar_mul(rstd[:, i:i + 1], rstd[:, i:i + 1], -0.5),
                        "V", 1)
            W("scalar", t4b)
            t5 = bd.do("scalar", lambda E, i=st_i:
                       E.activation(rstd[:, i:i + 1], rstd[:, i:i + 1], AF.Exp), "S", 1)
            W("vector", t5)
            t6 = bd.do("vector", lambda E, so=src_off, do_=dst_off, i=st_i,
                       s=src_sb, d=dst_sb:
                       E.tensor_scalar(d[:, do_:do_ + 1024], s[:, so:so + 1024],
                                       mv[:, 2 * i:2 * i + 1], rstd[:, i:i + 1],
                                       ALU.subtract, ALU.mult), "V", 1)
            return t6

        def transpose_subtile(src_sb, src_off, dst_sb, FT, col_off, v_dep, reuse_tok):
            ev = None
            for c in range(8):
                W("tensor", v_dep, pv4_last[c % 4])
                tt = bd.do("tensor", lambda E, c=c, o=src_off, s=src_sb:
                           E.transpose(pv4[c % 4][:, 0:128],
                                       s[:, o + 128 * c:o + 128 * c + 128],
                                       ident[:]), "T", 1)
                W("scalar", tt, reuse_tok if c == 0 else None)
                ev = bd.do("scalar", lambda E, c=c, co=col_off, d=dst_sb, FT=FT:
                           E.copy(d[:, c * FT + co:c * FT + co + 128],
                                  pv4[c % 4][:, 0:128]), "S", 1)
                pv4_last[c % 4] = ev
            return ev

        for a in range(NA):
            W("sync", last["xa"])
            dx = []
            for ss in range(2):
                dx.append(bd.do("sync", lambda E, ss=ss, a=a:
                                E.dma_start(out=xa[:, 1024 * ss:1024 * ss + 1024],
                                            in_=x_att[256 * a + 128 * ss:
                                                      256 * a + 128 * ss + 128, :]),
                                "X", 16))
            t_xnf = None
            for ss in range(2):
                t_ln = ln_subtile(xa, 1024 * ss, xa, 1024 * ss, ss,
                                  [dx[1], last["xa"]])
                t_xnf = transpose_subtile(xa, 1024 * ss, xnf, TA, 128 * ss,
                                          t_ln, last["xnf"] if ss == 0 else None)
            last["xa"] = None

            t_shift = {"q": None, "k": None}
            t_evac_qk = None
            for oc in range(16):
                sl = oc % 2
                W("sync", t_wb_prev[sl])
                dw = bd.do("sync", lambda E, oc=oc, sl=sl:
                           E.dma_start(out=wb[:, 1024 * sl:1024 * sl + 1024],
                                       in_=wqk[oc]), "W%d" % sl, 16)
                W("tensor", dw, t_xnf, last["pmm%d" % sl])
                mm = None
                for kc in range(KC):
                    mm = bd.do("tensor", lambda E, kc=kc, sl=sl:
                               E.matmul(pmm[sl][:, 0:TA],
                                        wb[:, 1024 * sl + 128 * kc:
                                           1024 * sl + 128 * kc + 128],
                                        xnf[:, TA * kc:TA * kc + TA],
                                        start=(kc == 0), stop=(kc == KC - 1)),
                               "T", 1 if kc == KC - 1 else 0)
                t_wb_prev[sl] = mm
                dst = q_sr if oc < 8 else k_sr
                c = oc % 8
                W("scalar", mm,
                  last["q_sr"] if oc == 0 else None,
                  t_shift["q"] if oc == 8 else None,
                  last["oddst"] if oc == 0 else None)
                if oc < 8:
                    W("vector", mm,
                      last["q_sr"] if oc == 0 else None,
                      last["oddst"] if oc == 0 else None)
                    e1 = bd.do("vector", lambda E, c=c, sl=sl, dst=dst:
                               E.tensor_scalar_mul(
                                   dst.rearrange("p (t2 h t) -> p t2 h t", t2=2, h=16)
                                      [0:64, :, 2 * c, :],
                                   pmm[sl][0:64, 0:TA].rearrange("p (t2 t) -> p t2 t", t2=2),
                                   0.125), "V", 1)
                    e2 = bd.do("vector", lambda E, c=c, sl=sl:
                               E.tensor_scalar_mul(
                                   oddst.rearrange("p (t2 c t) -> p t2 c t", t2=2, c=8)
                                        [64:128, :, c, :],
                                   pmm[sl][64:128, 0:TA].rearrange("p (t2 t) -> p t2 t", t2=2),
                                   0.125), "V", 1)
                else:
                    e1 = bd.do("scalar", lambda E, c=c, sl=sl, dst=dst:
                               E.copy(dst.rearrange("p (t2 h t) -> p t2 h t", t2=2, h=16)
                                         [0:64, :, 2 * c, :],
                                      pmm[sl][0:64, 0:TA].rearrange("p (t2 t) -> p t2 t", t2=2)),
                               "S", 1)
                    e2 = bd.do("scalar", lambda E, c=c, sl=sl:
                               E.copy(oddst.rearrange("p (t2 c t) -> p t2 c t", t2=2, c=8)
                                           [64:128, :, c, :],
                                      pmm[sl][64:128, 0:TA].rearrange("p (t2 t) -> p t2 t", t2=2)),
                               "S", 1)
                last["pmm%d" % sl] = e2
                t_evac_qk = e2
                if oc in (7, 15):
                    key = "q" if oc == 7 else "k"
                    W("gpsimd", e2, last["q_sr"])
                    t_shift[key] = bd.do("gpsimd", lambda E, dst=dst:
                                         E.dma_start(
                                             out=dst.rearrange("p (t2 h t) -> p t2 h t",
                                                               t2=2, h=16)[0:64, :, 1::2, :],
                                             in_=oddst[64:128, :]), "SH", 16)
            last["oddst"] = t_shift["k"]

            t_vmm = None
            for kc in range(KC):
                sl = kc % 2
                W("sync", t_wv_prev[sl])
                dwv = bd.do("sync", lambda E, kc=kc, sl=sl:
                            E.dma_start(out=wvb[:, 1024 * sl:1024 * sl + 1024],
                                        in_=wv[kc]), "WV%d" % sl, 16)
                W("tensor", dwv)
                if kc == 0:
                    W("tensor", *pv4_last)
                    W("tensor", last["v_tok"])
                for tsub in range(2):
                    for vc in range(2):
                        t_vmm = bd.do("tensor", lambda E, kc=kc, sl=sl, tsub=tsub, vc=vc:
                                      E.matmul(pv4[tsub * 2 + vc][:, 0:512],
                                               xnf[:, TA * kc + 128 * tsub:
                                                   TA * kc + 128 * tsub + 128],
                                               wvb[:, 1024 * sl + 512 * vc:
                                                   1024 * sl + 512 * vc + 512],
                                               start=(kc == 0), stop=(kc == KC - 1)),
                                      "T", 1 if (tsub == 1 and vc == 1) else 0)
                t_wv_prev[sl] = t_vmm
            W("scalar", t_vmm, last["v_tok"])
            for tsub in range(2):
                for vc in range(2):
                    tv = bd.do("scalar", lambda E, tsub=tsub, vc=vc:
                               E.copy(v_tok[:, 1024 * tsub + 512 * vc:
                                             1024 * tsub + 512 * vc + 512],
                                      pv4[tsub * 2 + vc][:, 0:512]), "S", 1)
                    pv4_last[tsub * 2 + vc] = tv
            t_vev = tv
            W("gpsimd", t_vev, v_ones, last["v_pv"])
            t_vpv = None
            for g in range(16):
                for tsub in range(2):
                    t_vpv = bd.do("gpsimd", lambda E, g=g, tsub=tsub:
                                  E.dma_start(
                                      out=bass.AP(_raw(AR),
                                                  8 * g * ARF + c_vpv + 1040 * tsub,
                                                  [[ARF, 8], [65, 16], [1, 64]]),
                                      in_=bass.AP(_raw(AR),
                                                  g * 64 + c_vtok + 1024 * tsub,
                                                  [[ARF, 128], [1, 64]])),
                                  "PV", 16)
            last["v_tok"] = t_vpv

            for tt in range(2):
                for b4 in range(4):
                    W("tensor", t_shift["q"], t_shift["k"], t_evac_qk,
                      last["p3"], t_vpv)
                    mmS = None
                    for slq in range(4):
                        vi = 4 * b4 + slq
                        mmS = bd.do("tensor", lambda E, vi=vi, slq=slq, tt=tt:
                                    E.matmul(p3[:, 128 * slq:128 * slq + 128],
                                             k_sr.rearrange("p (t2 g t) -> p t2 g t",
                                                            t2=2, g=16)
                                                 [:, tt, :, vi:vi + 113:16],
                                             q_sr.rearrange("p (t2 h t) -> p t2 h t",
                                                            t2=2, h=16)
                                                 [:, tt, :, vi:vi + 113:16],
                                             start=True, stop=True),
                                    "T", 1 if slq == 3 else 0)
                    W("scalar", mmS, last["Pt"])
                    t_exp = bd.do("scalar", lambda E:
                                  E.activation(Pt[:], p3[:], AF.Exp), "S", 1)
                    last["p3"] = t_exp
                    W("tensor", t_exp, last["p2"])
                    mmPV = None
                    for slq in range(4):
                        vi = 4 * b4 + slq
                        mmPV = bd.do("tensor", lambda E, vi=vi, slq=slq, tt=tt:
                                     E.matmul(p2[:, 65 * slq:65 * slq + 65],
                                              Pt[:, 128 * slq:128 * slq + 128],
                                              v_pv[:, 1040 * tt + 65 * vi:
                                                   1040 * tt + 65 * vi + 65],
                                              start=True, stop=True),
                                     "T", 1 if slq == 3 else 0)
                    last["Pt"] = mmPV
                    W("vector", mmPV, last["o_sb"] if b4 == 0 else None)
                    tr1 = bd.do("vector", lambda E:
                                E.reciprocal(rzb[:], bass.AP(_raw(p2), 64,
                                                             [[512, 128], [65, 4]])),
                                "V", 1)
                    W("vector", tr1)
                    t_norm = bd.do("vector", lambda E, b4=b4:
                                   E.tensor_tensor(
                                       o_sb.rearrange("p (v hd) -> p v hd", hd=64)
                                           [:, 4 * b4:4 * b4 + 4, :],
                                       bass.AP(_raw(p2), 0, [[512, 128], [65, 4], [1, 64]]),
                                       rzb[:].unsqueeze(2).broadcast_to([128, 4, 64]),
                                       ALU.mult), "V", 1)
                    last["p2"] = t_norm
                for b in range(8):
                    W("tensor", t_norm, pv4_last[b % 4])
                    ttr = bd.do("tensor", lambda E, b=b:
                                E.transpose(pv4[b % 4][:, 0:128],
                                            o_sb[:, 128 * b:128 * b + 128], ident[:]),
                                "T", 1)
                    W("vector", ttr, last["o_mq"] if (tt == 0 and b == 0) else None)
                    tev = bd.do("vector", lambda E, b=b, tt=tt:
                                E.tensor_copy(
                                    o_mq.rearrange("p (m hx) -> p m hx", m=8)[:, b, :]
                                        .rearrange("p (h j) -> p h j", h=16)
                                        [:, :, 8 * tt:8 * tt + 8],
                                    pv4[b % 4][:, 0:128]
                                        .rearrange("p (h j) -> p h j", h=16)),
                                "V", 1)
                    pv4_last[b % 4] = tev
                last["o_sb"] = bd.tok("T")
            last["v_pv"] = bd.tok("T")

            t_omq_ev = bd.tok("V")
            for oc in range(8):
                sl = oc % 2
                W("sync", t_wb_prev[sl])
                dwp = bd.do("sync", lambda E, oc=oc, sl=sl:
                            E.dma_start(out=wb[:, 1024 * sl:1024 * sl + 1024],
                                        in_=wproj[oc]), "W%d" % sl, 16)
                W("tensor", dwp, t_omq_ev if oc == 0 else None, last["pmm%d" % sl])
                mmp = None
                for m in range(8):
                    mmp = bd.do("tensor", lambda E, m=m, sl=sl:
                                E.matmul(pmm[sl][:, 0:TA],
                                         wb[:, 1024 * sl + 128 * m:
                                            1024 * sl + 128 * m + 128],
                                         o_mq[:, TA * m:TA * m + TA],
                                         start=(m == 0), stop=(m == 7)),
                                "T", 1 if m == 7 else 0)
                t_wb_prev[sl] = mmp
                W("vector", mmp)
                tpe = bd.do("vector", lambda E, oc=oc, sl=sl, a=a:
                            E.tensor_scalar(
                                acc.rearrange("p (o r) -> p o r", o=8)[:, oc, :]
                                   .rearrange("p (h g) -> p h g", h=16)
                                   [:, :, 16 * a:16 * a + 16],
                                pmm[sl][:, 0:TA].rearrange("p (h g) -> p h g", h=16),
                                bproj_sb[:, oc:oc + 1], None, ALU.add), "V", 1)
                last["pmm%d" % sl] = tpe
            last["o_mq"] = bd.tok("T")
            last["xnf"] = bd.tok("T")
            last["q_sr"] = bd.tok("T")
            last["xa"] = bd.tok("T")

        t_acc_done = bd.tok("V")

        for f in range(NF):
            W("sync", last["xr"])
            dxr = []
            for tsub in range(4):
                dxr.append(bd.do("sync", lambda E, f=f, tsub=tsub:
                                 E.dma_start(out=xr[:, 1024 * tsub:1024 * tsub + 1024],
                                             in_=x_res[512 * f + 128 * tsub:
                                                       512 * f + 128 * tsub + 128, :]),
                                 "X", 16))
            t_xnff = None
            for tsub in range(4):
                xsl = tsub % 2
                t_ln = ln_subtile(xr, 1024 * tsub, xnr, 1024 * xsl, tsub,
                                  [dxr[3], last["xnr%d" % xsl], last["xr"]])
                t_xnff = transpose_subtile(xnr, 1024 * xsl, xnff, TF, 128 * tsub,
                                           t_ln, last["xnff"] if tsub == 0 else None)
                last["xnr%d" % xsl] = bd.tok("T")

            W("tensor", t_xnff, last["p2"])
            mlg = None
            for tsub in range(4):
                for kc in range(KC):
                    mlg = bd.do("tensor", lambda E, tsub=tsub, kc=kc:
                                E.matmul(p2[:, 8 * tsub:8 * tsub + 8],
                                         xnff[:, TF * kc + 128 * tsub:
                                              TF * kc + 128 * tsub + 128],
                                         gate_sb[:, 8 * kc:8 * kc + 8],
                                         start=(kc == 0), stop=(kc == KC - 1)),
                                "T", 1 if (kc == KC - 1 and tsub == 3) else 0)
            W("vector", mlg)
            tlg = bd.do("vector", lambda E: E.tensor_copy(lgt[:], p2[:, 0:32]), "V", 1)
            last["p2"] = tlg
            W("vector", tlg)
            ts0 = bd.do("vector", lambda E:
                        E.tensor_scalar_mul(rt0[:], lgt[:], -1.0), "V", 1)
            W("scalar", ts0)
            ts1 = bd.do("scalar", lambda E:
                        E.activation(rt0[:], rt0[:], AF.Exp), "S", 1)
            W("vector", ts1)
            bd.do("vector", lambda E: E.tensor_scalar_add(rt0[:], rt0[:], 1.0), "V", 1)
            W("vector", bd.tok("V"))
            tv2 = bd.do("vector", lambda E: E.reciprocal(rt0[:], rt0[:]), "V", 1)
            W("vector", tv2)
            tv3 = bd.do("vector", lambda E:
                        E.tensor_tensor(rt2.rearrange("p (t e) -> p t e", t=4),
                                        rt0.rearrange("p (t e) -> p t e", t=4),
                                        gbias_sb[:].unsqueeze(1).broadcast_to([128, 4, 8]),
                                        ALU.add), "V", 1)
            g3 = rt2.rearrange("p (t g e) -> p t g e", t=4, g=2)
            r8 = rt1.rearrange("p (k x) -> p k x", k=8)

            def rr(i):
                return r8[:, i, :].rearrange("p (t g) -> p t g", t=4)

            def tt_op(dst_off, in0, in1, op):
                return bd.do("vector", lambda E, d=dst_off, a0=in0, a1=in1, op=op:
                             E.tensor_tensor(rr(d) if True else None, a0, a1, op), "V", 1)
            W("vector", tv3)
            tt_op(0, g3[:, :, :, 0], g3[:, :, :, 1], ALU.max)
            tt_op(1, g3[:, :, :, 0], g3[:, :, :, 1], ALU.min)
            tt_op(2, g3[:, :, :, 2], g3[:, :, :, 3], ALU.max)
            td_ = tt_op(3, g3[:, :, :, 2], g3[:, :, :, 3], ALU.min)
            W("vector", td_)
            tt_op(4, rr(0), rr(2), ALU.max)
            tt_op(5, rr(0), rr(2), ALU.min)
            tg_ = tt_op(6, rr(1), rr(3), ALU.max)
            W("vector", tg_)
            th_ = tt_op(5, rr(5), rr(6), ALU.max)
            W("vector", th_)
            ti_ = tt_op(4, rr(4), rr(5), ALU.add)
            gs = rr(4)
            W("vector", ti_)
            tj_ = bd.do("vector", lambda E:
                        E.tensor_tensor(rt1[:, 48:52], gs[:, :, 0], gs[:, :, 1],
                                        ALU.is_ge), "V", 1)
            W("vector", tj_)
            tk_ = bd.do("vector", lambda E:
                        E.tensor_scalar(rt1[:, 52:56], rt1[:, 48:52], -1.0, 1.0,
                                        ALU.mult, ALU.add), "V", 1)
            W("vector", tk_)
            bd.do("vector", lambda E:
                  E.tensor_tensor(rt3.rearrange("p (t g e) -> p t g e", t=4, g=2)
                                  [:, :, 0, :],
                                  g3[:, :, 0, :],
                                  rt1[:, 48:52].unsqueeze(2).broadcast_to([128, 4, 4]),
                                  ALU.mult), "V", 1)
            tm1 = bd.do("vector", lambda E:
                        E.tensor_tensor(rt3.rearrange("p (t g e) -> p t g e", t=4, g=2)
                                        [:, :, 1, :],
                                        g3[:, :, 1, :],
                                        rt1[:, 52:56].unsqueeze(2).broadcast_to([128, 4, 4]),
                                        ALU.mult), "V", 1)
            W("vector", tm1)
            bd.do("vector", lambda E: E.memset(rt1[:, 57:64], -2e30), "V", 1)
            W("vector", bd.tok("V"))
            t_cw = None
            for ts in range(4):
                sl8 = slice(8 * ts, 8 * ts + 8)
                t1_ = bd.do("vector", lambda E, s=sl8:
                            E.reduce_max(rt1[:, 56:57], rt3[:, s], AX.X), "V", 1)
                W("vector", t1_)
                t2_ = bd.do("vector", lambda E, s=sl8:
                            E.match_replace(rt4[:, s], rt1[:, 56:64],
                                            rt3[:, s], -1e30), "V", 1)
                W("vector", t2_)
                t3_ = bd.do("vector", lambda E, s=sl8:
                            E.reduce_max(rt1[:, 56:57], rt4[:, s], AX.X), "V", 1)
                W("vector", t3_)
                t4_ = bd.do("vector", lambda E, s=sl8:
                            E.match_replace(rt5[:, s], rt1[:, 56:64],
                                            rt4[:, s], -1e30), "V", 1)
                W("vector", t4_)
                t5_ = bd.do("vector", lambda E, s=sl8:
                            E.tensor_tensor(rt5[:, s], rt4[:, s], rt5[:, s],
                                            ALU.not_equal), "V", 1)
                W("vector", t5_)
                t6_ = bd.do("vector", lambda E, s=sl8:
                            E.tensor_tensor(rt4[:, s], rt3[:, s], rt4[:, s],
                                            ALU.not_equal), "V", 1)
                W("vector", t6_)
                t7_ = bd.do("vector", lambda E, s=sl8:
                            E.tensor_tensor(rt4[:, s], rt4[:, s], rt5[:, s],
                                            ALU.add), "V", 1)
                W("vector", t7_)
                t8_ = bd.do("vector", lambda E, s=sl8:
                            E.tensor_tensor(rt4[:, s], rt0[:, s], rt4[:, s],
                                            ALU.mult), "V", 1)
                W("vector", t8_)
                t9_ = bd.do("vector", lambda E, s=sl8, ts=ts:
                            E.reduce_sum(rt1[:, 48 + ts:49 + ts], rt4[:, s], AX.X),
                            "V", 1)
                W("vector", t9_)
                ta2 = bd.do("vector", lambda E, ts=ts:
                            E.tensor_scalar_add(rt1[:, 48 + ts:49 + ts],
                                                rt1[:, 48 + ts:49 + ts], 1e-20), "V", 1)
                W("vector", ta2)
                tb2 = bd.do("vector", lambda E, ts=ts:
                            E.reciprocal(rt1[:, 48 + ts:49 + ts],
                                         rt1[:, 48 + ts:49 + ts]), "V", 1)
                W("vector", tb2)
                tc2 = bd.do("vector", lambda E, ts=ts:
                            E.tensor_scalar_mul(rt1[:, 48 + ts:49 + ts],
                                                rt1[:, 48 + ts:49 + ts], 2.5), "V", 1)
                W("vector", tc2)
                t_cw = bd.do("vector", lambda E, s=sl8, ts=ts:
                             E.tensor_scalar_mul(rt2[:, s], rt4[:, s],
                                                 rt1[:, 48 + ts:49 + ts]), "V", 1)
            t_cwfm = None
            for ts in range(4):
                W("tensor", t_cw, pv4_last[ts], last["cwfm"] if ts == 0 else None)
                tct = bd.do("tensor", lambda E, ts=ts:
                            E.transpose(pv4[ts][0:8, 0:128], rt2[:, 8 * ts:8 * ts + 8],
                                        ident[:]), "T", 1)
                W("vector", tct)
                t_cwfm = bd.do("vector", lambda E, ts=ts:
                               E.tensor_copy(cwfm[0:8, 128 * ts:128 * ts + 128],
                                             pv4[ts][0:8, 0:128]), "V", 1)
                pv4_last[ts] = t_cwfm

            t_rout = [None] * 8
            for ih in range(2):
                t_sh_ev = None
                for ic8 in range(8):
                    ic = 8 * ih + ic8
                    sl = ic8 % 2
                    W("sync", t_wb_prev[sl])
                    dws = bd.do("sync", lambda E, ic=ic, sl=sl:
                                E.dma_start(out=wb[:, 1024 * sl:1024 * sl + 1024],
                                            in_=wsup[ic]), "W%d" % sl, 16)
                    W("tensor", dws, last["pmm%d" % sl])
                    mms = None
                    for kc in range(KC):
                        mms = bd.do("tensor", lambda E, kc=kc, sl=sl:
                                    E.matmul(pmm[sl][:, 0:TF],
                                             wb[:, 1024 * sl + 128 * kc:
                                                1024 * sl + 128 * kc + 128],
                                             xnff[:, TF * kc:TF * kc + TF],
                                             start=(kc == 0), stop=(kc == KC - 1)),
                                    "T", 1 if kc == KC - 1 else 0)
                    t_wb_prev[sl] = mms
                    W("vector", mms, last["scr"] if ic8 == 0 else None)
                    t_sh_r = bd.do("vector", lambda E, ic8=ic8, sl=sl:
                                   E.tensor_scalar_max(scr[:, TF * ic8:TF * ic8 + TF],
                                                       pmm[sl][:, 0:TF], 0.0), "V", 1)
                    W("scalar", t_sh_r)
                    t_sh_ev = bd.do("scalar", lambda E, ic8=ic8:
                                    E.square(scr[:, TF * ic8:TF * ic8 + TF],
                                             scr[:, TF * ic8:TF * ic8 + TF]), "S", 1)
                    last["pmm%d" % sl] = t_sh_r
                for oc in range(8):
                    sl = oc % 2
                    W("sync", t_wbig_prev[sl])
                    dwd = bd.do("sync", lambda E, oc=oc, ih=ih, sl=sl:
                                E.dma_start(out=wbig[:, 1024 * sl:1024 * sl + 1024],
                                            in_=wsdown[oc, ih]), "B%d" % sl, 16)
                    W("tensor", dwd, t_sh_ev, last["pmm%d" % sl])
                    mmd = None
                    for ic8 in range(8):
                        mmd = bd.do("tensor", lambda E, ic8=ic8, sl=sl:
                                    E.matmul(pmm[sl][:, 0:TF],
                                             wbig[:, 1024 * sl + 128 * ic8:
                                                  1024 * sl + 128 * ic8 + 128],
                                             scr[:, TF * ic8:TF * ic8 + TF],
                                             start=(ic8 == 0), stop=(ic8 == 7)),
                                    "T", 1 if ic8 == 7 else 0)
                    t_wbig_prev[sl] = mmd
                    if ih == 0:
                        W("vector", mmd, last["routed"] if oc == 0 else None)
                        trt = bd.do("vector", lambda E, oc=oc, sl=sl, f=f:
                                    E.tensor_tensor(routed[:, TF * oc:TF * oc + TF],
                                                    pmm[sl][:, 0:TF],
                                                    acc[:, 2048 * oc + TF * f:
                                                        2048 * oc + TF * f + TF],
                                                    ALU.add), "V", 1)
                    else:
                        W("vector", mmd, t_rout[oc])
                        trt = bd.do("vector", lambda E, oc=oc, sl=sl:
                                    E.tensor_tensor(routed[:, TF * oc:TF * oc + TF],
                                                    pmm[sl][:, 0:TF],
                                                    routed[:, TF * oc:TF * oc + TF],
                                                    ALU.add), "V", 1)
                    t_rout[oc] = trt
                    last["pmm%d" % sl] = trt
                last["scr"] = bd.tok("T")

            for ep in range(4):
                t_r2 = [None] * 8
                for e2 in range(2):
                    e = 2 * ep + e2
                    for ic in range(4):
                        sl = ic % 2
                        W("sync", t_wb_prev[sl])
                        dwu = bd.do("sync", lambda E, e=e, ic=ic, sl=sl:
                                    E.dma_start(out=wb[:, 1024 * sl:1024 * sl + 1024],
                                                in_=wup[e, ic]), "W%d" % sl, 16)
                        W("tensor", dwu, last["pmm%d" % sl])
                        mmu = None
                        for kc in range(KC):
                            mmu = bd.do("tensor", lambda E, kc=kc, sl=sl:
                                        E.matmul(pmm[sl][:, 0:TF],
                                                 wb[:, 1024 * sl + 128 * kc:
                                                    1024 * sl + 128 * kc + 128],
                                                 xnff[:, TF * kc:TF * kc + TF],
                                                 start=(kc == 0), stop=(kc == KC - 1)),
                                        "T", 1 if kc == KC - 1 else 0)
                        t_wb_prev[sl] = mmu
                        ch = 4 * e2 + ic
                        W("vector", mmu,
                          last["scr"] if (e2 == 0 and ic == 0) else None)
                        tre0 = bd.do("vector", lambda E, ch=ch, sl=sl:
                                     E.tensor_scalar_max(scr[:, TF * ch:TF * ch + TF],
                                                         pmm[sl][:, 0:TF], 0.0), "V", 1)
                        W("scalar", tre0)
                        tre = bd.do("scalar", lambda E, ch=ch:
                                    E.square(scr[:, TF * ch:TF * ch + TF],
                                             scr[:, TF * ch:TF * ch + TF]), "S", 1)
                        t_r2[ch] = tre
                        last["pmm%d" % sl] = tre0
                    W("tensor", t_cwfm, last["p2"])
                    mcw = bd.do("tensor", lambda E, e=e:
                                E.matmul(p2[:, 0:TF], selm_sb[:, 128 * e:128 * e + 128],
                                         cwfm[0:8, :], start=True, stop=True), "T", 1)
                    for ic in range(4):
                        ch = 4 * e2 + ic
                        W("vector", mcw, t_r2[ch])
                        tcs = bd.do("vector", lambda E, ch=ch:
                                    E.tensor_tensor(scr[:, TF * ch:TF * ch + TF],
                                                    scr[:, TF * ch:TF * ch + TF],
                                                    p2[:, 0:TF], ALU.mult), "V", 1)
                        t_r2[ch] = tcs
                    last["p2"] = t_r2[4 * e2 + 3]
                for oc in range(8):
                    sl = oc % 2
                    W("sync", t_wbig_prev[sl])
                    dwd = bd.do("sync", lambda E, oc=oc, ep=ep, sl=sl:
                                E.dma_start(out=wbig[:, 1024 * sl:1024 * sl + 1024],
                                            in_=wdown[oc, ep]), "B%d" % sl, 16)
                    W("tensor", dwd, last["pmm%d" % sl])
                    if oc == 0:
                        W("tensor", *t_r2)
                    mmd = None
                    for ch in range(8):
                        mmd = bd.do("tensor", lambda E, ch=ch, sl=sl:
                                    E.matmul(pmm[sl][:, 0:TF],
                                             wbig[:, 1024 * sl + 128 * ch:
                                                  1024 * sl + 128 * ch + 128],
                                             scr[:, TF * ch:TF * ch + TF],
                                             start=(ch == 0), stop=(ch == 7)),
                                    "T", 1 if ch == 7 else 0)
                    t_wbig_prev[sl] = mmd
                    W("vector", mmd, t_rout[oc])
                    trt = bd.do("vector", lambda E, oc=oc, sl=sl:
                                E.tensor_tensor(routed[:, TF * oc:TF * oc + TF],
                                                pmm[sl][:, 0:TF],
                                                routed[:, TF * oc:TF * oc + TF],
                                                ALU.add), "V", 1)
                    t_rout[oc] = trt
                    last["pmm%d" % sl] = trt
                last["scr"] = bd.tok("T")
            last["cwfm"] = bd.tok("T")

            for tsub in range(4):
                osl = tsub % 2
                tfe = None
                for oc in range(8):
                    W("tensor", t_rout[oc] if tsub == 0 else None, last["p3"])
                    tft = bd.do("tensor", lambda E, oc=oc, tsub=tsub:
                                E.transpose(p3[:, 0:128],
                                            routed[:, TF * oc + 128 * tsub:
                                                   TF * oc + 128 * tsub + 128],
                                            ident[:]), "T", 1)
                    W("vector", tft, t_outp_prev[osl] if oc == 0 else None)
                    tfe = bd.do("vector", lambda E, oc=oc, tsub=tsub, osl=osl:
                                E.tensor_tensor(
                                    outp[:, 1024 * osl + 128 * oc:
                                         1024 * osl + 128 * oc + 128],
                                    p3[:, 0:128],
                                    xr[:, 1024 * tsub + 128 * oc:
                                       1024 * tsub + 128 * oc + 128],
                                    ALU.add), "V", 1)
                    last["p3"] = tfe
                W("gpsimd", tfe)
                t_outp_prev[osl] = bd.do("gpsimd", lambda E, f=f, tsub=tsub, osl=osl:
                                         E.dma_start(
                                             out=out_d[512 * f + 128 * tsub:
                                                       512 * f + 128 * tsub + 128, :],
                                             in_=outp[:, 1024 * osl:1024 * osl + 1024]),
                                         "O%d" % (osl), 16)
            last["xr"] = bd.tok("V")
            last["routed"] = bd.tok("T")

        for k in ("I", "X", "W0", "W1", "WV0", "WV1", "B0", "B1", "SH", "PV",
                  "O0", "O1"):
            bd.W("gpsimd", (k, bd.cnt[k]))

        @block.sync
        def _(E):
            bd.replay("sync", E, sems)

        @block.gpsimd
        def _(E):
            bd.replay("gpsimd", E, sems)

        @block.tensor
        def _(E):
            bd.replay("tensor", E, sems)

        @block.vector
        def _(E):
            bd.replay("vector", E, sems)

        @block.scalar
        def _(E):
            bd.replay("scalar", E, sems)

    return nc


_NC = None


def _get_nc():
    global _NC
    if _NC is None:
        _NC = build_nc()
    return _NC


def _prep_shared(inputs):
    f32 = np.float32
    w_qkv = np.asarray(inputs["L0_attn_w_qkv"], f32)
    w_proj = np.asarray(inputs["L0_attn_w_proj"], f32)
    b_proj = np.asarray(inputs["L0_attn_b_proj"], f32)
    gate_w = np.asarray(inputs["L0_ffn_gate_weight"], f32)
    gate_b = np.asarray(inputs["L0_ffn_gate_bias"], f32)
    up = np.asarray(inputs["L0_ffn_up_weights"], f32)
    down = np.asarray(inputs["L0_ffn_down_weights"], f32)
    sup = np.asarray(inputs["L0_ffn_shared_up_weight"], f32)
    sdown = np.asarray(inputs["L0_ffn_shared_down_weight"], f32)

    C = np.ascontiguousarray
    sh = {}
    sh["wqk"] = C(w_qkv[:2048].reshape(16, 128, 8, 128).transpose(0, 3, 2, 1)
                  .reshape(16, 128, 1024))
    sh["wv"] = C(w_qkv[2048:].reshape(1024, 8, 128).transpose(1, 2, 0))
    sh["wproj"] = C(w_proj.reshape(8, 128, 8, 128).transpose(0, 3, 2, 1)
                    .reshape(8, 128, 1024))
    sh["wup"] = C(up.reshape(8, 4, 128, 8, 128).transpose(0, 1, 4, 3, 2)
                  .reshape(8, 4, 128, 1024))
    sh["wdown"] = C(down.reshape(4, 2, 8, 128, 4, 128).transpose(2, 0, 5, 1, 4, 3)
                    .reshape(8, 4, 128, 1024))
    sh["wsup"] = C(sup.reshape(16, 128, 8, 128).transpose(0, 3, 2, 1)
                   .reshape(16, 128, 1024))
    sh["wsdown"] = C(sdown.reshape(8, 128, 2, 8, 128).transpose(0, 2, 4, 3, 1)
                     .reshape(8, 2, 128, 1024))
    sh["wgate"] = C(gate_w.T.reshape(8, 128, 8).transpose(1, 0, 2).reshape(128, 64))
    sh["bproj"] = C(b_proj.reshape(8, 128).T)
    sh["gbias"] = C(np.broadcast_to(gate_b, (128, 8)))
    sh["identw"] = np.eye(128, dtype=f32)
    jpat = ((np.arange(16 * TA) % 128) // 16) % 8
    km = np.zeros((8, 16 * TA), f32)
    qm = np.zeros((8, 16 * TA), f32)
    for j0 in range(8):
        km[j0] = (jpat == j0).astype(f32)
        qm[j0] = -MASKC * (jpat != j0).astype(f32)
    sh["kmask"] = km
    sh["qmask"] = qm
    selmat = np.zeros((8, 1024), f32)
    for e in range(8):
        selmat[e, 128 * e:128 * e + 128] = 1.0
    sh["selm"] = selmat
    return sh


def kernel(**inputs):
    x = np.asarray(inputs["L0_attn_x"], np.float32)
    sh = _prep_shared(inputs)
    rr = 512 * np.arange(16)[:, None] + np.arange(128)[None, :]
    in_maps = []
    for c in range(8):
        b, jc = c // 4, c % 4
        m = dict(sh)
        m["x_att"] = np.ascontiguousarray(x[b, 2048 * jc:2048 * jc + 2048])
        m["x_res"] = np.ascontiguousarray(x[b][(rr + 128 * jc).ravel()])
        in_maps.append(m)
    nc = _get_nc()
    res = run_bass_kernel_spmd(nc, in_maps, core_ids=list(range(8)))
    out = np.empty((2, 8192, 1024), np.float32)
    for c in range(8):
        b, jc = c // 4, c % 4
        out[b][(rr + 128 * jc).ravel()] = res.results[c]["out"]
    return out
